# version 04 (final)
# baseline (speedup 1.0000x reference)
import numpy as np
from contextlib import ExitStack

import concourse.bass as bass
import concourse.mybir as mybir
from concourse.bass_utils import run_bass_kernel_spmd

F32 = mybir.dt.float32
BF16 = mybir.dt.bfloat16
I32 = mybir.dt.int32
AF = mybir.ActivationFunctionType
ALU = mybir.AluOpType
AX = mybir.AxisListType.X

TOK = 1024
D = 2048
II = 8192
KC = 16
NIC = 64
H = 16
M = 64
EPS_LN = 1e-5
EPS_AT = 1e-6

N_STAGES = 3
DBG = None


class K:
    ENGS = ("sync", "scalar", "vector", "tensor", "gpsimd")

    def __init__(self, nc):
        self.nc = nc
        self.ops = {e: [] for e in self.ENGS}
        self.ctx = ExitStack()
        self.sem = {}
        for name in ("dS", "dA", "dG", "dT", "cP", "cA", "cV", "cG"):
            self.sem[name] = self.ctx.enter_context(nc.semaphore(name))
        self.cnt = {n: 0 for n in self.sem}
        self.dmasem = {"sync": "dS", "scalar": "dA", "gpsimd": "dG"}
        self.csem = {"tensor": "cP", "scalar": "cA", "vector": "cV", "gpsimd": "cG"}
        self.hw = {e: {} for e in self.ENGS}

    DMA_SEMS = ("dS", "dA", "dG", "dT")

    def _waits(self, eng, waits):
        wl = []
        seen = set()
        dedup = eng != "gpsimd"
        for w in waits:
            if w is None:
                continue
            s, v = w
            if s in self.DMA_SEMS:
                v = self.cnt[s]
            if dedup:
                if self.hw[eng].get(s, 0) >= v:
                    continue
                self.hw[eng][s] = v
            else:
                if (s, v) in seen:
                    continue
                seen.add((s, v))
            wl.append((self.sem[s], v))
        return wl

    def emit(self, eng, build, waits=()):
        sname = self.csem[eng]
        self.cnt[sname] += 1
        val = self.cnt[sname]
        sem = self.sem[sname]
        wl = self._waits(eng, waits)

        def f(h, build=build, wl=wl, sem=sem):
            for (s, v) in wl:
                h.wait_ge(s, v)
            build(h).then_inc(sem, 1)
        self.ops[eng].append(f)
        return (sname, val)

    def dma(self, eng, out, in_, waits=(), **kw):
        sname = "dT" if kw.get("transpose") else self.dmasem[eng]
        sem = self.sem[sname]
        wl = self._waits(eng, waits)
        self.cnt[sname] += 16
        val = self.cnt[sname]

        def f(h, out=out, in_=in_, wl=wl, sem=sem, kw=kw):
            for (s, v) in wl:
                h.wait_ge(s, v)
            h.dma_start(out=out, in_=in_, **kw).then_inc(sem, 16)
        self.ops[eng].append(f)
        return (sname, val)

    def fence(self, eng, fsrc, fdst):
        if eng == "vector":
            return self.emit(eng, lambda v: v.tensor_copy(fdst, fsrc))
        if eng == "scalar":
            return self.emit(eng, lambda s: s.copy(fdst, fsrc))
        raise ValueError(eng)

    def wait(self, eng, waits):
        wl = self._waits(eng, waits)
        if wl:
            def f(h, wl=wl):
                for (s, v) in wl:
                    h.wait_ge(s, v)
            self.ops[eng].append(f)

    def finalize(self):
        block = self.ctx.enter_context(self.nc.Block())
        for eng in self.ENGS:
            lst = self.ops[eng]
            sec = getattr(block, eng)

            def runner(h, lst=lst):
                for f in lst:
                    f(h)
            sec(runner)


def build_program():
    nc = bass.Bass("TRN2", target_bir_lowering=False)
    ap = lambda t: t.ap()

    x_in = ap(nc.dram_tensor("x_in", [TOK, D], F32, kind="ExternalInput"))
    gw = ap(nc.dram_tensor("gw", [II, D], I32, kind="ExternalInput"))
    uw = ap(nc.dram_tensor("uw", [II, D], I32, kind="ExternalInput"))
    dw = ap(nc.dram_tensor("dw", [D, II], I32, kind="ExternalInput"))
    gs_l = ap(nc.dram_tensor("gs_l", [128, NIC], F32, kind="ExternalInput"))
    us_l = ap(nc.dram_tensor("us_l", [128, NIC], F32, kind="ExternalInput"))
    ds_l = ap(nc.dram_tensor("ds_l", [128, KC], F32, kind="ExternalInput"))
    wq = ap(nc.dram_tensor("wq", [D, D], F32, kind="ExternalInput"))
    wk = ap(nc.dram_tensor("wk", [D, D], F32, kind="ExternalInput"))
    wv = ap(nc.dram_tensor("wv", [D, D], F32, kind="ExternalInput"))
    wo = ap(nc.dram_tensor("wo", [D, D], F32, kind="ExternalInput"))
    wmq_t = ap(nc.dram_tensor("wmq_t", [128, M], F32, kind="ExternalInput"))
    wmk_t = ap(nc.dram_tensor("wmk_t", [128, M], F32, kind="ExternalInput"))
    bq_l = ap(nc.dram_tensor("bq_l", [128, H], F32, kind="ExternalInput"))
    bk_l = ap(nc.dram_tensor("bk_l", [128, H], F32, kind="ExternalInput"))
    bv_r = ap(nc.dram_tensor("bv_r", [1, D], F32, kind="ExternalInput"))
    bo_r = ap(nc.dram_tensor("bo_r", [1, D], F32, kind="ExternalInput"))
    ident_in = ap(nc.dram_tensor("ident", [128, 128], F32, kind="ExternalInput"))
    out_ext = ap(nc.dram_tensor("out", [TOK, D], F32, kind="ExternalOutput"))
    x2_out = ap(nc.dram_tensor("x2_out", [TOK, D], F32, kind="ExternalOutput"))

    natG = ap(nc.dram_tensor("natG", [KC, II, 128], BF16))
    natU = ap(nc.dram_tensor("natU", [KC, II, 128], BF16))
    natD = ap(nc.dram_tensor("natD", [NIC, D, 128], BF16))
    natQ = ap(nc.dram_tensor("natQ", [D, D], BF16))
    natK = ap(nc.dram_tensor("natK", [D, D], BF16))
    natV = ap(nc.dram_tensor("natV", [KC, D, 128], BF16))
    natO = ap(nc.dram_tensor("natO", [KC, D, 128], BF16))
    ctG = ap(nc.dram_tensor("ctG", [NIC, 128, KC * 128], BF16))
    ctU = ap(nc.dram_tensor("ctU", [NIC, 128, KC * 128], BF16))
    ctD = ap(nc.dram_tensor("ctD", [4, 8, 128, 4096], BF16))
    xbuf = ap(nc.dram_tensor("xbuf", [TOK, D], F32))
    ccin = ap(nc.dram_tensor("ccin", [H, M, 129], F32))
    ccout = ap(nc.dram_tensor("ccout", [H, M, 129], F32))

    k = K(nc)
    AR_N = 53100
    ar = k.ctx.enter_context(nc.sbuf_tensor("ar", [128, AR_N], F32))[:, :]
    ps = k.ctx.enter_context(nc.psum_tensor("ps", [128, 4096], F32))[:, :]

    off = [0]

    def carve(n, align=8):
        o = (off[0] + align - 1) // align * align
        off[0] = o + n
        assert off[0] <= AR_N, f"arena overflow {off[0]}"
        return o

    o = carve(8192)
    lnfm = ar[:, o:o + 8192].bitcast(BF16)
    lnfm_k = lambda kk: lnfm[:, kk * 1024:(kk + 1) * 1024]

    o = carve(NIC); gs = ar[:, o:o + NIC]
    o = carve(NIC); us = ar[:, o:o + NIC]
    o = carve(KC); ds = ar[:, o:o + KC]
    o = carve(NIC); gb = ar[:, o:o + NIC]
    o = carve(NIC); ub = ar[:, o:o + NIC]
    o = carve(KC); db = ar[:, o:o + KC]
    o = carve(KC); dsh = ar[:, o:o + KC]
    o = carve(128); ident = ar[:, o:o + 128]
    o = carve(32); wmq_b = ar[:, o:o + 32].bitcast(BF16)
    o = carve(32); wmk_b = ar[:, o:o + 32].bitcast(BF16)
    o = carve(8); bqb = ar[:, o:o + 8].bitcast(BF16)
    o = carve(8); bkb = ar[:, o:o + 8].bitcast(BF16)
    o = carve(64); sm = ar[:, o:o + 64]
    o = carve(1280)
    bv_b = ar[0:1, o:o + 1024].bitcast(BF16)
    bo_b = ar[32:33, o:o + 1024].bitcast(BF16)
    bqm_r = ar[64:65, o + 256:o + 768].bitcast(BF16)
    bkm_r = ar[64:65, o + 768:o + 1280].bitcast(BF16)
    ones64 = ar[64:65, o:o + 256].bitcast(BF16)
    ones0 = ar[0:1, o + 1024:o + 1088].bitcast(BF16)
    ones32 = ar[32:33, o + 1024:o + 1088].bitcast(BF16)

    o_wgu = carve(4096)
    wgu = ar[:, o_wgu:o_wgu + 4096].bitcast(BF16)
    wgu_buf = lambda i: wgu[:, i * 2048:(i + 1) * 2048]
    wd_buf = lambda i: wgu[:, i * 4096:(i + 1) * 4096]
    kvsb = ar[:, o_wgu:o_wgu + 2048]
    kvsb4 = lambda q: kvsb[:, q * 512:(q + 1) * 512]
    qwstg2 = lambda i: wgu[:, 4096 + (i % 2) * 2048:4096 + ((i % 2) + 1) * 2048]

    o_xstg = carve(2048)
    xstg = ar[:, o_xstg:o_xstg + 2048]
    xst3 = lambda i: xstg[:, (i % 2) * 1024:((i % 2) + 1) * 1024]
    rb2 = lambda i: xstg[:, (i % 2) * 1024:((i % 2) + 1) * 1024]
    rowf = xstg[0:1, 0:2048]

    o_lntm = carve(2048)
    lntm = ar[:, o_lntm:o_lntm + 2048]
    lntm2 = lambda i: lntm[:, (i % 2) * 1024:((i % 2) + 1) * 1024]
    nr_f = lntm[0:1, 0:1024]
    nr_b = lntm[0:1, 1024:1536].bitcast(BF16)

    o_sev = carve(2048)
    sev = ar[:, o_sev:o_sev + 2048]
    silu2 = lambda i: sev[:, (i % 2) * 512:((i % 2) + 1) * 512].bitcast(BF16)
    evx = lambda i: lntm[:, (i % 2) * 512:((i % 2) + 1) * 512]
    evr = lambda i: sev[:, 1024 + (i % 2) * 512:1024 + ((i % 2) + 1) * 512]

    o_escr = o_sev
    escr = ar[:, o_escr:o_escr + 2048]

    o_H = carve(32768)
    HR = ar[:, o_H:o_H + 32768]
    h1_tile = lambda ic, s: HR[:, ic * 256:ic * 256 + 256].bitcast(BF16)
    h_tile = lambda ic: HR[:, ic * 512:(ic + 1) * 512].bitcast(BF16)
    bstg_i = lambda i: HR[:, 16384 + (i % 2) * 1024:16384 + ((i % 2) + 1) * 1024].bitcast(I32)
    bstg_f = lambda i: HR[:, 16384 + (i % 2) * 1024:16384 + ((i % 2) + 1) * 1024]
    cvt2 = lambda i: HR[:, 18432 + (i % 2) * 512:18432 + ((i % 2) + 1) * 512].bitcast(BF16)
    tstg2 = lambda i: HR[:, 19456 + (i % 2) * 1024:19456 + ((i % 2) + 1) * 1024].bitcast(BF16)
    wvT_k = lambda kk: HR[:, kk * 1024:(kk + 1) * 1024].bitcast(BF16)
    WMv = HR[:, 0:8192].bitcast(BF16)
    WM_k = lambda kk: WMv[:, kk * 1024:(kk + 1) * 1024]
    ao_k = lambda kk: HR[:, kk * 512:(kk + 1) * 512].bitcast(BF16)
    kf_t = lambda tc: HR[:, 8192 + tc * 512:8192 + (tc + 1) * 512].bitcast(BF16)
    qf_t = lambda oc: HR[:, 12288 + oc * 512:12288 + (oc + 1) * 512].bitcast(BF16)
    vtm = lambda tc: HR[:, 16384 + tc * 1040:16384 + tc * 1040 + 1040].bitcast(BF16)
    woT_k = lambda kk: HR[:, 16384 + kk * 1024:16384 + (kk + 1) * 1024].bitcast(BF16)
    kvf = lambda j: HR[:, 8192 + j * 144:8192 + j * 144 + 136]
    kvb = lambda j: ar[:, o_escr + j * 80:o_escr + j * 80 + 68].bitcast(BF16)

    bank = lambda b, n=1: ps[:, b * 512:(b + n) * 512]

    TEN, VEC, SCA, GPS, SYN = "tensor", "vector", "scalar", "gpsimd", "sync"

    t_ident = k.dma(SYN, ident, ident_in)
    t_gs = k.dma(SYN, gs, gs_l)
    t_us = k.dma(SYN, us, us_l)
    t_ds = k.dma(SYN, ds, ds_l)
    t_gb = k.emit(VEC, lambda v: v.tensor_scalar_mul(gb, gs, -128.0), waits=[t_gs])
    t_ub = k.emit(VEC, lambda v: v.tensor_scalar_mul(ub, us, -128.0), waits=[t_us])
    t_dsh = k.emit(VEC, lambda v: v.tensor_scalar_mul(dsh, ds, 0.5), waits=[t_ds])
    t_db = k.emit(VEC, lambda v: v.tensor_scalar_mul(db, dsh, -128.0), waits=[t_dsh])
    t_l1 = k.dma(SYN, sm[:, 0:64], wmq_t)
    t_c1 = k.emit(VEC, lambda v: v.tensor_copy(wmq_b, sm[:, 0:64]), waits=[t_l1])
    t_l2 = k.dma(SYN, sm[:, 0:64], wmk_t, waits=[t_c1])
    t_c2 = k.emit(VEC, lambda v: v.tensor_copy(wmk_b, sm[:, 0:64]), waits=[t_l2])
    t_l3 = k.dma(SYN, sm[:, 0:16], bq_l, waits=[t_c1])
    t_c3 = k.emit(VEC, lambda v: v.tensor_copy(bqb, sm[:, 0:16]), waits=[t_l3])
    t_l4 = k.dma(SYN, sm[:, 16:32], bk_l)
    t_c4 = k.emit(VEC, lambda v: v.tensor_copy(bkb, sm[:, 16:32]), waits=[t_l4])
    t_ones = k.emit(VEC, lambda v: v.memset(ones0, 1.0))
    t_ones = k.emit(VEC, lambda v: v.memset(ones32, 1.0))
    t_ones = k.emit(VEC, lambda v: v.memset(ones64, 1.0))
    epsc = sm[:, 40:41]
    f1 = sm[:, 44:45]; f2 = sm[:, 45:46]; f3 = sm[:, 46:47]; f4 = sm[:, 47:48]
    vfence = lambda: k.fence("vector", f1, f2)
    sfence = lambda: k.fence("scalar", f3, f4)
    t_eps = k.emit(VEC, lambda v: v.memset(epsc, EPS_LN))
    t_l5 = k.dma(SYN, rowf, bv_r)
    t_c5 = k.emit(VEC, lambda v: v.tensor_copy(bv_b, rowf), waits=[t_l5])
    t_l6 = k.dma(SYN, rowf, bo_r, waits=[t_c5])
    t_c6 = k.emit(VEC, lambda v: v.tensor_copy(bo_b, rowf), waits=[t_l6])

    S0, S1, Q0, Q1 = (sm[:, 32 + j:33 + j] for j in range(4))
    MEAN, VAR, RSTD, NB = (sm[:, 36 + j:37 + j] for j in range(4))

    def layer_norm(src_ap, free_toks, src_tok):
        free_toks = [f for f in (free_toks or [])]
        last_ev = None
        last_cons = [None, None]
        last_tr = [None, None]
        prev_stats = None
        for t in range(8):
            ld = []
            for hf in range(2):
                idx = 2 * t + hf
                tok = k.dma(SYN, xst3(idx),
                            src_ap[t * 128:(t + 1) * 128, hf * 1024:(hf + 1) * 1024],
                            waits=[last_cons[idx % 2], src_tok])
                ld.append(tok)
            w0 = [ld[0], prev_stats]
            w1 = [ld[1], prev_stats]
            t_s0 = k.emit(VEC, lambda v, b=xst3(2 * t): v.reduce_sum(S0, b, axis=AX), waits=w0)
            t_s1 = k.emit(VEC, lambda v, b=xst3(2 * t + 1): v.reduce_sum(S1, b, axis=AX), waits=w1)
            t_m = k.emit(VEC, lambda v: v.tensor_add(MEAN, S0, S1), waits=[t_s0, t_s1])
            t_m = k.emit(VEC, lambda v: v.tensor_scalar_mul(MEAN, MEAN, 1.0 / D), waits=[t_m])
            t_nm = k.emit(VEC, lambda v: v.tensor_scalar_mul(NB, MEAN, -1.0), waits=[t_m])
            t_nm = vfence()
            t_q0 = k.emit(SCA, lambda s, b=xst3(2 * t): s.activation(
                escr[:, 0:1024], b, AF.Square, bias=NB, accum_out=Q0),
                waits=w0 + free_toks + [t_nm])
            t_q1 = k.emit(SCA, lambda s, b=xst3(2 * t + 1): s.activation(
                escr[:, 1024:2048], b, AF.Square, bias=NB, accum_out=Q1),
                waits=w1 + [t_nm])
            t_v = k.emit(VEC, lambda v: v.tensor_add(VAR, Q0, Q1), waits=[t_q0, t_q1])
            t_v = k.emit(VEC, lambda v: v.tensor_scalar_mul(VAR, VAR, 1.0 / D), waits=[t_v])
            t_ln = k.emit(SCA, lambda s: s.activation(RSTD, VAR, AF.Ln, bias=epsc),
                          waits=[t_v, t_eps])
            t_rs = k.emit(SCA, lambda s: s.activation(VAR, RSTD, AF.Exp, scale=-0.5), waits=[t_ln])
            t_rs = sfence()
            t_nb = k.emit(VEC, lambda v: v.tensor_scalar(NB, MEAN, VAR, -1.0, ALU.mult, ALU.mult),
                          waits=[t_rs, t_m])
            t_nb = vfence()
            norm_toks = []
            for hf in range(2):
                idx = 2 * t + hf
                grp = idx % 2
                t_nrm = k.emit(SCA, lambda s, b=xst3(idx), o=lntm2(idx): s.activation(
                    o, b, AF.Identity, scale=VAR, bias=NB), waits=[t_nb, ld[hf], last_tr[grp]])
                t_nrm = sfence()
                norm_toks.append(t_nrm)
                t_tr = None
                for j in range(8):
                    t_tr = k.emit(TEN, lambda te, j=j, grp=grp, idx=idx: te.transpose(
                        ps[:, grp * 1024 + j * 128:grp * 1024 + (j + 1) * 128],
                        lntm2(idx)[:, j * 128:(j + 1) * 128], ident),
                        waits=[t_nrm, last_tr[grp], t_ident] + free_toks)
                dst = lnfm.rearrange("p (kk f) -> p kk f", kk=16)[
                    :, hf * 8:(hf + 1) * 8, t * 128:(t + 1) * 128]
                src = ps[:, grp * 1024:(grp + 1) * 1024].rearrange("p (kk f) -> p kk f", kk=8)
                t_ev = k.emit(VEC, lambda v, dst=dst, src=src: v.tensor_copy(dst, src),
                              waits=[t_tr] + free_toks)
                t_ev = vfence()
                last_tr[grp] = t_ev
                last_ev = t_ev
            last_cons[0] = norm_toks[0]
            last_cons[1] = norm_toks[1]
            prev_stats = norm_toks[1]
        return last_ev

    bu = [0]
    ts_war = [None, None]
    nat_war = [None, None]

    def nat_unit(w_slice, s_col, b_col, dst):
        u = bu[0]; bu[0] += 1
        t_ld = k.dma(SCA, bstg_i(u), w_slice, waits=[ts_war[u % 2]])
        t_cv = k.emit(VEC, lambda v, u=u, s_col=s_col, b_col=b_col: v.tensor_scalar(
            cvt2(u), bstg_i(u), s_col, b_col, ALU.mult, ALU.add),
            waits=[t_ld, t_gb, t_ub, t_db, nat_war[u % 2]])
        t_cv = vfence()
        ts_war[u % 2] = t_cv
        t_nat = k.dma(GPS, dst, cvt2(u).rearrange("p (c f) -> p c f", f=128), waits=[t_cv])
        nat_war[u % 2] = t_nat
        return t_nat

    def nat_gu_span(w_ap, nat_ap, s_ap, b_ap, sp):
        t = None
        for ic in range(sp * 16, (sp + 1) * 16):
            for half in range(2):
                dst = nat_ap[half * 8:(half + 1) * 8, ic * 128:(ic + 1) * 128, :].transpose([1, 0, 2])
                t = nat_unit(w_ap[ic * 128:(ic + 1) * 128, half * 1024:(half + 1) * 1024],
                             s_ap[:, ic:ic + 1], b_ap[:, ic:ic + 1], dst)
        return t

    def nat_d_quarter(q):
        t = None
        for hc in range(KC):
            for half in range(2):
                dst = natD[q * 16 + half * 8:q * 16 + (half + 1) * 8,
                           hc * 128:(hc + 1) * 128, :].transpose([1, 0, 2])
                t = nat_unit(dw[hc * 128:(hc + 1) * 128,
                                q * 2048 + half * 1024:q * 2048 + (half + 1) * 1024],
                             dsh[:, hc:hc + 1], db[:, hc:hc + 1], dst)
        return t

    def attn_unit(w_slice, dst):
        u = bu[0]; bu[0] += 1
        t_ld = k.dma(SCA, bstg_f(u), w_slice, waits=[ts_war[u % 2]])
        t_cv = k.emit(VEC, lambda v, u=u: v.tensor_copy(cvt2(u), bstg_f(u)),
                      waits=[t_ld, nat_war[u % 2]])
        t_cv = vfence()
        ts_war[u % 2] = t_cv
        t_nat = k.dma(GPS, dst, cvt2(u).rearrange("p (c f) -> p c f", f=128), waits=[t_cv])
        nat_war[u % 2] = t_nat
        return t_nat

    def attn_nat(w_ap, nat_ap, blocked):
        t = None
        for rb in range(KC):
            for half in range(2):
                if blocked:
                    dst = nat_ap[half * 8:(half + 1) * 8,
                                 rb * 128:(rb + 1) * 128, :].transpose([1, 0, 2])
                else:
                    dst = nat_ap[rb * 128:(rb + 1) * 128,
                                 half * 1024:(half + 1) * 1024].rearrange(
                                     "r (c f) -> r c f", f=128)
                t = attn_unit(w_ap[rb * 128:(rb + 1) * 128, half * 1024:(half + 1) * 1024], dst)
        return t

    t_ln1 = layer_norm(x_in, [], t_c6)

    if DBG == "ln":
        o_ln = None
        for q in range(4):
            tok = k.dma(SYN, out_ext[q * 128:(q + 1) * 128, :],
                        ar[:, 0 + q * 2048:0 + (q + 1) * 2048], waits=[t_ln1])
        fin = [(n, k.cnt[n]) for n in ("dS", "dA", "dG", "dT", "cP", "cA", "cV", "cG")]
        for eng in K.ENGS:
            k.wait(eng, fin)
        k.finalize()
        return nc

    ct_war = [None, None]
    ct_idx = [0]

    def ct_gu_span(nat_ap, ct_ap, nat_tok, sp):
        t_ct = None
        for kk in range(KC):
            i = ct_idx[0]; ct_idx[0] += 1
            t_xb = k.dma(SYN, tstg2(i), nat_ap[kk, sp * 2048:(sp + 1) * 2048, :],
                         waits=[nat_tok, ct_war[i % 2]], transpose=True)
            dst = ct_ap[sp * 16:(sp + 1) * 16, :, kk * 128:(kk + 1) * 128].transpose([1, 0, 2])
            t_ct = k.dma(SYN, dst, tstg2(i).rearrange("p (c f) -> p c f", f=128), waits=[t_xb])
            ct_war[i % 2] = t_ct
        return t_ct

    def ct_d_block(ic, nat_tok):
        i = ct_idx[0]; ct_idx[0] += 1
        t_xb = k.dma(SYN, tstg2(i), natD[ic, :, :],
                     waits=[nat_tok, ct_war[i % 2]], transpose=True)
        dst = ctD[:, ic // 8, :, (ic % 8) * 512:(ic % 8) * 512 + 512].transpose([1, 0, 2])
        t_ct = k.dma(SYN, dst, tstg2(i).rearrange("p (c f) -> p c f", f=512), waits=[t_xb])
        ct_war[i % 2] = t_ct
        return t_ct

    def ffn_gu(nsl, t_ln, gu_rdy, li_base, ev, wdma_war, emit_weights=True):
        pass

    def gu_ic(ic, nsl, t_ln, w_gu, ev_list, war_ld):
        assert nsl == 1
        par = ic % 2
        t_gw = k.dma(SYN, wgu_buf(2 * par), ctG[ic, :, :], waits=[w_gu, war_ld[par]])
        t_uw = k.dma(SYN, wgu_buf(2 * par + 1), ctU[ic, :, :], waits=[w_gu, war_ld[par]])
        gps_ = bank(4 * par, 2)
        ups_ = bank(4 * par + 2, 2)
        prev_ev = ev_list[-2] if len(ev_list) >= 2 else None
        pw = [prev_ev, t_ln, t_gw, t_uw]
        t_g = t_u = None
        for kk in range(KC):
            for r in range(2):
                rhs = lnfm_k(kk)[:, r * 512:(r + 1) * 512]
                po = r * 512
                t_g = k.emit(TEN, lambda te, kk=kk, po=po, par=par, rhs=rhs: te.matmul(
                    gps_[:, po:po + 512], wgu_buf(2 * par)[:, kk * 128:(kk + 1) * 128],
                    rhs, start=(kk == 0), stop=(kk == KC - 1)), waits=pw)
                t_u = k.emit(TEN, lambda te, kk=kk, po=po, par=par, rhs=rhs: te.matmul(
                    ups_[:, po:po + 512], wgu_buf(2 * par + 1)[:, kk * 128:(kk + 1) * 128],
                    rhs, start=(kk == 0), stop=(kk == KC - 1)), waits=pw)
        war_ld[par] = t_u
        t_si = k.emit(SCA, lambda sc: sc.activation(silu2(ic), gps_, AF.Silu),
                      waits=[t_g, prev_ev])
        t_h = k.emit(VEC, lambda v, ic=ic: v.tensor_tensor(
            h_tile(ic), ups_, silu2(ic), ALU.mult), waits=[t_u, t_si])
        ev_list.append(t_h)
        return t_h, t_u

    def gu_mm_half(ic, s, t_ln, w_gu, ev_list, war_ld):
        par = ic % 2
        t_gw = k.dma(SYN, wgu_buf(2 * par), ctG[ic, :, :], waits=[w_gu, war_ld[par]])
        t_uw = k.dma(SYN, wgu_buf(2 * par + 1), ctU[ic, :, :], waits=[w_gu, war_ld[par]])
        gps_ = bank(2 * par)
        ups_ = bank(2 * par + 1)
        prev_ev = ev_list[ic - 2] if ic >= 2 else None
        pw = [prev_ev, t_ln, t_gw, t_uw]
        t_g = t_u = None
        for kk in range(KC):
            rhs = lnfm_k(kk)[:, s * 512:(s + 1) * 512]
            t_g = k.emit(TEN, lambda te, kk=kk, par=par, rhs=rhs, gps_=gps_: te.matmul(
                gps_, wgu_buf(2 * par)[:, kk * 128:(kk + 1) * 128],
                rhs, start=(kk == 0), stop=(kk == KC - 1)), waits=pw)
            t_u = k.emit(TEN, lambda te, kk=kk, par=par, rhs=rhs, ups_=ups_: te.matmul(
                ups_, wgu_buf(2 * par + 1)[:, kk * 128:(kk + 1) * 128],
                rhs, start=(kk == 0), stop=(kk == KC - 1)), waits=pw)
        war_ld[par] = t_u
        return t_g, t_u

    def gu_ep_half(ic, s, t_g, t_u, ev_list, h_war, drain_tok):
        par = ic % 2
        gps_ = bank(2 * par)
        ups_ = bank(2 * par + 1)
        prev_ev = ev_list[ic - 2] if ic >= 2 else None
        t_si = k.emit(SCA, lambda sc, gps_=gps_, ic=ic: sc.activation(
            silu2(ic)[:, 0:512], gps_, AF.Silu), waits=[t_g, prev_ev, drain_tok])
        t_si = sfence()
        t_h = k.emit(VEC, lambda v, ic=ic, s=s, ups_=ups_: v.tensor_tensor(
            h1_tile(ic, s), ups_, silu2(ic)[:, 0:512], ALU.mult),
            waits=[t_u, t_si, h_war, drain_tok])
        t_h = vfence()
        ev_list.append(t_h)
        return t_h

    def gu_round(s, t_ln, w_gu_of, war_ld, h_war):
        ev_list = []
        pend = None
        last_u = None
        for ic in range(NIC):
            t_g, t_u = gu_mm_half(ic, s, t_ln, w_gu_of(ic), ev_list, war_ld)
            if pend is not None:
                gu_ep_half(pend[0], s, pend[1], pend[2], ev_list, h_war, t_g)
            pend = (ic, t_g, t_u)
            last_u = t_u
        gu_ep_half(pend[0], s, pend[1], pend[2], ev_list, h_war, None)
        return ev_list, last_u

    def ffn_down(nsl, src_ap, dst_ap, d_rdy, ev_list, gate_w, s_list=None,
                 bank0=0, h_of=None, state=None, xw=()):
        SL = TOK // nsl
        ntc = SL // 128
        st_toks = []
        if state is None:
            state = {"bank_last": {tc: ev_list[-1] for tc in range(ntc)},
                     "wq_war": [gate_w, gate_w], "xl_war": [None, None],
                     "res_war": [None, None]}
        else:
            for tc in range(ntc):
                state["bank_last"][tc] = state["bank_last"].get(tc) or ev_list[-1]
        bank_last = state["bank_last"]
        wq_war = state["wq_war"]
        xl_war = state["xl_war"]
        res_war = state["res_war"]
        if h_of is None:
            h_of = (lambda i, s: h_tile(i)[:, s * SL:(s + 1) * SL]) if nsl == 1 else h1_tile
        GRP = ntc
        for s in (s_list if s_list is not None else range(nsl)):
            for hq in range(4):
                last_mm = None
                for tc0 in range(0, ntc, GRP):
                    for ig in range(8):
                        t_dw = k.dma(SYN, wd_buf(ig % 2), ctD[hq, ig, :, :],
                                     waits=[d_rdy(ig) if d_rdy else None, wq_war[ig % 2]] + list(xw))
                        for r in range(8):
                            i = ig * 8 + r
                            for tc in range(tc0, min(tc0 + GRP, ntc)):
                                last_mm = k.emit(TEN, lambda te, i=i, tc=tc, r=r, s=s, ig=ig: te.matmul(
                                    bank(bank0 + tc), h_of(i, s)[:, tc * 128:(tc + 1) * 128],
                                    wd_buf(ig % 2)[:, r * 512:(r + 1) * 512],
                                    start=(i == 0), stop=(i == NIC - 1)),
                                    waits=[ev_list[i], t_dw, bank_last[tc]] + list(xw))
                        wq_war[ig % 2] = last_mm
                for tc in range(ntc):
                    par = tc % 2
                    rows = slice(s * SL + tc * 128, s * SL + (tc + 1) * 128)
                    cols = slice(hq * 512, (hq + 1) * 512)
                    t_xl = k.dma(SYN, evx(tc), src_ap[rows, cols], waits=[xl_war[par]])
                    t_ad = k.emit(VEC, lambda v, tc=tc: v.tensor_tensor(
                        evr(tc), bank(bank0 + tc), evx(tc), ALU.add),
                        waits=[last_mm, t_xl, res_war[par]])
                    t_ad = k.emit(VEC, lambda v: v.tensor_copy(sm[:, 47:48], sm[:, 46:47]),
                                  waits=[])
                    t_st = k.dma(SCA, dst_ap[rows, cols], evr(tc), waits=[t_ad])
                    bank_last[tc] = t_ad
                    xl_war[par] = t_ad
                    res_war[par] = t_st
                    st_toks.append(t_st)
        state["last_mm"] = last_mm
        return st_toks

    war_ld1 = [None, None]
    span_toks = []
    for sp in range(4):
        tng = nat_gu_span(gw, natG, gs, gb, sp)
        tnu = nat_gu_span(uw, natU, us, ub, sp)
        tg = ct_gu_span(natG, ctG, tng, sp)
        tu = ct_gu_span(natU, ctU, tnu, sp)
        span_toks.append(tu)
    ev1, gu_last_u = gu_round(0, t_ln1, lambda ic: span_toks[ic // 16], war_ld1, None)
    d_blk = []
    for q in range(4):
        tnd = nat_d_quarter(q)
        for ic in range(q * 16, (q + 1) * 16):
            d_blk.append(ct_d_block(ic, tnd))
    t_natV = attn_nat(wv, natV, True)
    t_natO = attn_nat(wo, natO, True)
    t_natQ = attn_nat(wq, natQ, False)
    t_natK = attn_nat(wk, natK, False)

    if DBG == "build":
        gate = [gu_last_u, d_blk[-1], ("dG", k.cnt["dG"]), ("cV", k.cnt["cV"]),
                ("dA", k.cnt["dA"])]
        tok = k.dma(SYN, out_ext[0:128, 0:1024].bitcast(BF16), ctG[0, :, :], waits=gate)
        tok = k.dma(SYN, out_ext[128:256, 0:1024].bitcast(BF16), ctU[0, :, :])
        for q in range(4):
            tok = k.dma(SYN, out_ext[256 + 128 * q:384 + 128 * q, 0:2048].bitcast(BF16),
                        ctD[q, 0, :, 0:4096])
        for b_ in range(16):
            tok = k.dma(SYN, out_ext[768 + 8 * b_:776 + 8 * b_, 0:1024].bitcast(BF16),
                        natD[0, b_ * 128:(b_ + 1) * 128, :].rearrange("(a b) c -> a (b c)", a=8))
        tok = k.dma(SYN, out_ext[896:1024, 0:64].bitcast(BF16), natG[0, 0:128, :])
        fin = [(n, k.cnt[n]) for n in ("dS", "dA", "dG", "dT", "cP", "cA", "cV", "cG")]
        for eng in K.ENGS:
            k.wait(eng, fin)
        k.finalize()
        return nc

    if DBG == "gu":
        for q in range(8):
            tok = k.dma(SYN, out_ext[q * 128:(q + 1) * 128, :].bitcast(BF16),
                        HR[:, q * 2048:(q + 1) * 2048].bitcast(BF16), waits=[ev1[-1]])
        fin = [(n, k.cnt[n]) for n in ("dS", "dA", "dG", "dT", "cP", "cA", "cV", "cG")]
        for eng in K.ENGS:
            k.wait(eng, fin)
        k.finalize()
        return nc
    dst1 = xbuf if N_STAGES > 1 else out_ext
    dstate = {"bank_last": {}, "wq_war": [gu_last_u, gu_last_u],
              "xl_war": [t_ln1, t_ln1], "res_war": [None, None]}
    for tc in range(4):
        dstate["bank_last"][tc] = None
    def phase_barrier():
        toks = [(n, k.cnt[n]) for n in ("dS", "dA", "dG", "dT", "cP", "cA", "cV", "cG")]
        for eng in K.ENGS:
            k.wait(eng, toks)

    phase_barrier()
    ffn1_st = ffn_down(2, x_in, dst1, lambda ig: d_blk[ig * 8 + 7], ev1, gu_last_u,
                       s_list=[0], bank0=4, state=dstate)
    down_s0_mm = dstate["last_mm"]
    phase_barrier()
    war_ld1 = [down_s0_mm, down_s0_mm]
    ev1b, gu_last_u = gu_round(1, t_ln1, lambda ic: None, war_ld1, down_s0_mm)
    dstate["wq_war"] = [gu_last_u, gu_last_u]
    phase_barrier()
    ffn1_st2 = ffn_down(2, x_in, dst1, None, ev1b, gu_last_u,
                        s_list=[1], bank0=4, state=dstate)
    t_ffn1 = ffn1_st2[-1]
    t_ffn1_mm = ("cP", k.cnt["cP"])

    if N_STAGES == 1:
        for tq in range(8):
            k.dma(SYN, x2_out[tq * 128:(tq + 1) * 128, :],
                  HR[:, tq * 2048:(tq + 1) * 2048], waits=[t_ffn1])
        fin = [(n, k.cnt[n]) for n in ("dS", "dA", "dG", "dT", "cP", "cA", "cV", "cG")]
        for eng in K.ENGS:
            k.wait(eng, fin)
        k.finalize()
        return nc

    t_ln2 = layer_norm(xbuf, [t_ffn1_mm, t_ffn1], t_ffn1)

    phase_barrier()
    wvT_toks = []
    for kk in range(KC):
        wvT_toks.append(k.dma(SYN, wvT_k(kk), natV[kk, :, :], transpose=True,
                              waits=[t_natV, t_ffn1_mm]))
    vev = {}
    t_build_gate = t_natK
    t_build_gate2 = d_blk[-1]
    vmm_last = None
    for tc in range(8):
        pb = bank(4 * (tc % 2), 4)
        stw = [t_ln2, t_c5, t_ones, wvT_toks[-1], vev.get(tc - 2)]
        t_mm = None
        for r in range(4):
            t_mm = k.emit(TEN, lambda te, r=r, pb=pb: te.matmul(
                pb[:, r * 512:(r + 1) * 512], ones0[0:1, 0:128],
                bv_b[0:1, r * 512:(r + 1) * 512], start=True, stop=False), waits=stw)
        for kk in range(KC):
            for r in range(4):
                t_mm = k.emit(TEN, lambda te, kk=kk, r=r, pb=pb, tc=tc: te.matmul(
                    pb[:, r * 512:(r + 1) * 512], lnfm_k(kk)[:, tc * 128:(tc + 1) * 128],
                    wvT_k(kk)[:, r * 512:(r + 1) * 512], start=False, stop=(kk == KC - 1)),
                    waits=stw)
        vmm_last = t_mm
        dst = vtm(tc).rearrange("p (h f) -> p h f", h=16)[:, :, 0:128]
        t_ev = k.emit(VEC, lambda v, pb=pb, dst=dst: v.tensor_copy(
            dst, pb.rearrange("p (h f) -> p h f", h=16)),
            waits=[t_mm, t_build_gate, t_build_gate2])
        t_on = k.emit(VEC, lambda v, tc=tc: v.memset(
            vtm(tc).rearrange("p (h f) -> p h f", h=16)[:, :, 128:130], 1.0),
            waits=[t_ffn1_mm, t_build_gate, t_build_gate2])
        t_on = vfence()
        vev[tc] = t_on
        t_v_ev_last = t_on

    def build_bM(nat_b, out_row, wm, gate):
        toks = []
        for h in range(H):
            toks.append(k.emit(TEN, lambda te, h=h: te.matmul(
                ps[0:1, 2048 + (h % 8) * 64 + (h // 8) * 512:
                   2048 + (h % 8) * 64 + (h // 8) * 512 + 64],
                nat_b[:, h:h + 1], wm, start=True, stop=True),
                waits=[t_c2, t_c3, t_c4, gate]))
        t1 = k.emit(VEC, lambda v: v.tensor_copy(out_row[0:1, 0:512], ps[0:1, 2048:2560]),
                    waits=[toks[7]])
        t2 = k.emit(VEC, lambda v: v.tensor_copy(out_row[0:1, 512:1024], ps[0:1, 2560:3072]),
                    waits=[toks[15], t1])
        t2 = vfence()
        return t2

    def build_WM(nat_w, wm, nat_tok, free_tok):
        evt = None
        stg_war = [free_tok, free_tok]
        for h in range(H):
            t_st = k.dma(SYN, qwstg2(h), nat_w[h * 128:(h + 1) * 128, :],
                         waits=[nat_tok, stg_war[h % 2]])
            toks = []
            for kk in range(KC):
                t_mm = k.emit(TEN, lambda te, kk=kk, h=h: te.matmul(
                    ps[:, 3072 + (kk % 8) * 64 + (kk // 8) * 512:
                       3072 + (kk % 8) * 64 + (kk // 8) * 512 + 64],
                    qwstg2(h)[:, kk * 128:(kk + 1) * 128], wm,
                    start=True, stop=True), waits=[t_st, t_c1, t_c2, evt, free_tok])
                toks.append(t_mm)
            stg_war[h % 2] = toks[-1]
            for half in range(2):
                d = WMv.rearrange("p (kk f) -> p kk f", kk=16)[
                    :, half * 8:(half + 1) * 8, h * 64:(h + 1) * 64]
                src = ps[:, 3072 + half * 512:3072 + (half + 1) * 512].rearrange(
                    "p (kk f) -> p kk f", kk=8)
                evt = k.emit(VEC, lambda v, d=d, src=src: v.tensor_copy(d, src),
                             waits=[toks[half * 8 + 7], free_tok])
                evt = vfence()
        return evt

    phase_barrier()
    t_bkM = build_bM(bkb, bkm_r, wmk_b, t_v_ev_last)
    t_WkM = build_WM(natK, wmk_b, t_natK, t_v_ev_last)
    kf_toks = {}
    e2_prev = None
    km_last = None
    for tc in range(8):
        pb = bank(2 * (tc % 2), 2)
        stw = [t_WkM, t_bkM, t_ln2, kf_toks.get(tc - 2)]
        t_mm = None
        for r in range(2):
            t_mm = k.emit(TEN, lambda te, r=r, pb=pb: te.matmul(
                pb[:, r * 512:(r + 1) * 512], ones64[0:1, 0:128],
                bkm_r[0:1, r * 512:(r + 1) * 512], start=True, stop=False), waits=stw)
        for kk in range(KC):
            for r in range(2):
                t_mm = k.emit(TEN, lambda te, kk=kk, r=r, pb=pb, tc=tc: te.matmul(
                    pb[:, r * 512:(r + 1) * 512], lnfm_k(kk)[:, tc * 128:(tc + 1) * 128],
                    WM_k(kk)[:, r * 512:(r + 1) * 512], start=False, stop=(kk == KC - 1)),
                    waits=stw)
        km_last = t_mm
        t_e1 = k.emit(VEC, lambda v, pb=pb: v.tensor_scalar_min(escr[:, 0:1024], pb, 0.0),
                      waits=[t_mm, e2_prev])
        t_e2 = k.emit(SCA, lambda s: s.activation(escr[:, 1024:2048], escr[:, 0:1024], AF.Exp),
                      waits=[t_e1])
        t_e2 = sfence()
        t_e3 = k.emit(VEC, lambda v, pb=pb, tc=tc: v.scalar_tensor_tensor(
            kf_t(tc), pb, 0.0, escr[:, 1024:2048], ALU.max, ALU.add), waits=[t_e2])
        t_e3 = vfence()
        kf_toks[tc] = t_e3
        e2_prev = t_e3

    phase_barrier()
    t_bqM = build_bM(bqb, bqm_r, wmq_b, e2_prev)
    t_WqM = build_WM(natQ, wmq_b, t_natQ, km_last)
    qf_toks = {}
    qm_last = None
    for oc in range(8):
        pb = bank(2 * (oc % 2), 2)
        stw = [t_WqM, t_bqM, t_ln2, qf_toks.get(oc - 2)]
        t_mm = None
        for r in range(2):
            t_mm = k.emit(TEN, lambda te, r=r, pb=pb, oc=oc: te.matmul(
                pb[:, r * 512:(r + 1) * 512], bqm_r[0:1, oc * 128:(oc + 1) * 128],
                ones64[0:1, 0:512], start=True, stop=False), waits=stw)
        for kk in range(KC):
            for r in range(2):
                t_mm = k.emit(TEN, lambda te, kk=kk, r=r, pb=pb, oc=oc: te.matmul(
                    pb[:, r * 512:(r + 1) * 512], WM_k(kk)[:, oc * 128:(oc + 1) * 128],
                    lnfm_k(kk)[:, r * 512:(r + 1) * 512], start=False, stop=(kk == KC - 1)),
                    waits=stw)
        qm_last = t_mm
        t_e1 = k.emit(VEC, lambda v, pb=pb: v.tensor_scalar_min(escr[:, 0:1024], pb, 0.0),
                      waits=[t_mm, e2_prev])
        t_e2 = k.emit(SCA, lambda s: s.activation(escr[:, 1024:2048], escr[:, 0:1024], AF.Exp),
                      waits=[t_e1])
        t_e2 = sfence()
        t_e3 = k.emit(VEC, lambda v, pb=pb, oc=oc: v.scalar_tensor_tensor(
            qf_t(oc), pb, 0.0, escr[:, 1024:2048], ALU.max, ALU.add), waits=[t_e2])
        t_e3 = vfence()
        qf_toks[oc] = t_e3
        e2_prev = t_e3

    phase_barrier()
    kv_last = {}
    for h in range(H):
        grp, pb_, fo = h // 4, 64 * (h % 2), 256 * ((h % 4) // 2)
        t_mm = None
        for tc in range(8):
            t_mm = k.emit(TEN, lambda te, h=h, tc=tc, grp=grp, pb_=pb_, fo=fo: te.matmul(
                ps[pb_:pb_ + 64, grp * 512 + fo:grp * 512 + fo + 130],
                kf_t(tc)[:, h * 64:(h + 1) * 64], vtm(tc)[:, h * 130:h * 130 + 130],
                start=(tc == 0), stop=(tc == 7)),
                waits=[kf_toks[tc], vev[tc], qf_toks[7]])
        kv_last[h] = t_mm
    kv_ev = []
    for q in range(4):
        k.emit(VEC, lambda v, q=q: v.tensor_copy(kvsb4(q), bank(q)),
               waits=[kv_last[H - 1], t_ffn1])
        kv_ev.append(vfence())
    cc_dmas = []
    for h in range(H):
        grp, pb_, fo = h // 4, 64 * (h % 2), 256 * ((h % 4) // 2)
        cc_dmas.append(k.dma(GPS, ccin[h, :, :], kvsb4(grp)[pb_:pb_ + 64, fo:fo + 129],
                             waits=[kv_ev[grp]]))
    t_cc = k.emit(GPS, lambda g: g.collective_compute(
        "AllReduce", ALU.add, replica_groups=[[0, 1, 2, 3], [4, 5, 6, 7]],
        ins=[ccin.opt()], outs=[ccout.opt()]), waits=[cc_dmas[-1]])
    kvl_toks = []
    for j in range(8):
        kvl_toks.append(k.dma(SYN, kvf(j)[:, 0:129],
                              ccout[2 * j:2 * j + 2, :, :].rearrange("a b c -> (a b) c"),
                              waits=[t_cc]))
    kvb_toks = []
    for j in range(8):
        k.emit(VEC, lambda v, j=j: v.tensor_copy(
            kvb(j)[:, 0:129], kvf(j)[:, 0:129]), waits=[kvl_toks[j], e2_prev])
        kvb_toks.append(vfence())

    phase_barrier()
    ao_toks = {}
    rb_rel = [None, None]
    nr_rel = None
    for h in range(H):
        j, pb_ = h // 2, 64 * (h % 2)
        par = h % 2
        apb = bank(2 * par, 2)
        wm = [kvb_toks[j], qf_toks[h // 2], ao_toks.get(h - 2), kv_ev[3]]
        t_ao = None
        for r in range(2):
            t_ao = k.emit(TEN, lambda te, r=r, j=j, pb_=pb_, h=h, apb=apb: te.matmul(
                apb[:, r * 512:(r + 1) * 512], kvb(j)[pb_:pb_ + 64, 0:128],
                qf_t(h // 2)[pb_:pb_ + 64, r * 512:(r + 1) * 512],
                start=True, stop=True), waits=wm)
        t_nr = None
        for r in range(2):
            t_nr = k.emit(TEN, lambda te, r=r, j=j, pb_=pb_: te.matmul(
                ps[0:1, 2048 + r * 512:2048 + (r + 1) * 512], kvb(j)[pb_:pb_ + 64, 128:129],
                qf_t(j)[pb_:pb_ + 64, r * 512:(r + 1) * 512],
                start=True, stop=True), waits=wm + [nr_rel, t_bqM])
        t_n1 = k.emit(VEC, lambda v: v.tensor_scalar_add(nr_f, ps[0:1, 2048:3072], EPS_AT),
                      waits=[t_nr])
        t_n2 = k.emit(VEC, lambda v: v.reciprocal(nr_f, nr_f), waits=[t_n1])
        t_n3 = k.emit(VEC, lambda v: v.tensor_copy(nr_b, nr_f), waits=[t_n2])
        t_n3 = vfence()
        nr_rel = t_n3
        t_rb = None
        for r in range(2):
            t_rb = k.emit(TEN, lambda te, r=r: te.matmul(
                ps[:, 3072 + r * 512:3072 + (r + 1) * 512], ones0[0:1, 0:128],
                nr_b[0:1, r * 512:(r + 1) * 512], start=True, stop=True),
                waits=[t_n3, rb_rel[par], t_WqM])
        t_rbe = k.emit(VEC, lambda v, par=par: v.tensor_copy(rb2(par), ps[:, 3072:4096]),
                       waits=[t_rb])
        t_aoe = k.emit(VEC, lambda v, h=h, par=par, apb=apb: v.tensor_tensor(
            ao_k(h), apb, rb2(par), ALU.mult), waits=[t_rbe, t_ao, kv_ev[3]])
        t_aoe = vfence()
        ao_toks[h] = t_aoe
        rb_rel[par] = t_aoe

    phase_barrier()
    woT_toks = []
    for kk in range(KC):
        woT_toks.append(k.dma(SYN, woT_k(kk), natO[kk, :, :], transpose=True,
                              waits=[t_natO, kv_last[H - 1]]))
    wo_st = []
    xl_war = [None, None]
    st_war = [None, None]
    wo_mm_last = None
    for tc in range(8):
        par = tc % 2
        pb = bank(4 * par, 4)
        stw = [woT_toks[-1], t_c6, ao_toks[H - 1],
               wo_st[-2] if len(wo_st) >= 2 else None]
        t_mm = None
        for r in range(4):
            t_mm = k.emit(TEN, lambda te, r=r, pb=pb: te.matmul(
                pb[:, r * 512:(r + 1) * 512], ones32[0:1, 0:128],
                bo_b[0:1, r * 512:(r + 1) * 512], start=True, stop=False), waits=stw)
        for kk in range(KC):
            for r in range(4):
                t_mm = k.emit(TEN, lambda te, kk=kk, r=r, pb=pb, tc=tc: te.matmul(
                    pb[:, r * 512:(r + 1) * 512], ao_k(kk)[:, tc * 128:(tc + 1) * 128],
                    woT_k(kk)[:, r * 512:(r + 1) * 512], start=False, stop=(kk == KC - 1)),
                    waits=stw)
        wo_mm_last = t_mm
        for half in range(2):
            t_xl = k.dma(SYN, lntm2(half),
                         xbuf[tc * 128:(tc + 1) * 128, half * 1024:(half + 1) * 1024],
                         waits=[xl_war[half], t_ffn1, ao_toks[H - 1], nr_rel])
            t_ad = k.emit(VEC, lambda v, half=half, pb=pb: v.tensor_tensor(
                escr[:, half * 1024:(half + 1) * 1024], pb[:, half * 1024:(half + 1) * 1024],
                lntm2(half), ALU.add), waits=[t_mm, t_xl, st_war[half], kvb_toks[7]])
            t_ad = vfence()
            t_st = k.dma(SCA, xbuf[tc * 128:(tc + 1) * 128, half * 1024:(half + 1) * 1024],
                         escr[:, half * 1024:(half + 1) * 1024], waits=[t_ad])
            xl_war[half] = t_ad
            st_war[half] = t_st
            wo_st.append(t_st)
    t_attn = wo_st[-1]

    if N_STAGES == 2:
        for t in range(8):
            k.dma(SYN, out_ext[t * 128:(t + 1) * 128, :], xbuf[t * 128:(t + 1) * 128, :],
                  waits=[t_attn])
        fin = [(n, k.cnt[n]) for n in ("dS", "dA", "dG", "dT", "cP", "cA", "cV", "cG")]
        for eng in K.ENGS:
            k.wait(eng, fin)
        k.finalize()
        return nc

    for tq in range(8):
        k.dma(SYN, x2_out[tq * 128:(tq + 1) * 128, :], xbuf[tq * 128:(tq + 1) * 128, :],
              waits=[t_attn])
    phase_barrier()
    t_ln3 = layer_norm(xbuf, [qm_last, t_attn], t_attn)
    war_ld3 = [wo_mm_last, wo_mm_last]
    ev3, gu3_last = gu_round(0, t_ln3, lambda ic: wo_mm_last, war_ld3, wo_mm_last)
    dstate3 = {"bank_last": {}, "wq_war": [gu3_last, gu3_last],
               "xl_war": [t_ln3, t_ln3], "res_war": [None, None]}
    for tc in range(4):
        dstate3["bank_last"][tc] = None
    phase_barrier()
    ffn3_st = ffn_down(2, xbuf, out_ext, None, ev3, gu3_last, s_list=[0],
                       bank0=4, state=dstate3)
    down3_s0 = dstate3["last_mm"]
    phase_barrier()
    war_ld3b = [down3_s0, down3_s0]
    ev3b, gu3b_last = gu_round(1, t_ln3, lambda ic: None, war_ld3b, down3_s0)
    dstate3["wq_war"] = [gu3b_last, gu3b_last]
    phase_barrier()
    ffn3_st2 = ffn_down(2, xbuf, out_ext, None, ev3b, gu3b_last, s_list=[1],
                        bank0=4, state=dstate3)
    ffn3_st = ffn3_st + ffn3_st2

    fin = [(n, k.cnt[n]) for n in ("dS", "dA", "dG", "dT", "cP", "cA", "cV", "cG")]
    for eng in K.ENGS:
        k.wait(eng, fin)
    k.finalize()
    return nc


_NC = None


def _get_nc():
    global _NC
    if _NC is None:
        _NC = build_program()
    return _NC


def kernel(**inputs):
    x = np.ascontiguousarray(np.asarray(inputs["L0_attn_hidden_states"], dtype=np.float32))
    B, S, _ = x.shape
    xf = x.reshape(B * S, D)
    c = np.ascontiguousarray
    gsc = np.asarray(inputs["L0_ffn_gate_scale"], dtype=np.float32)
    usc = np.asarray(inputs["L0_ffn_up_scale"], dtype=np.float32)
    dsc = np.asarray(inputs["L0_ffn_down_scale"], dtype=np.float32)
    base = {
        "gw": c(np.asarray(inputs["L0_ffn_gate_w"], dtype=np.int32)),
        "uw": c(np.asarray(inputs["L0_ffn_up_w"], dtype=np.int32)),
        "dw": c(np.asarray(inputs["L0_ffn_down_w"], dtype=np.int32)),
        "gs_l": c(gsc.reshape(NIC, 128).T), "us_l": c(usc.reshape(NIC, 128).T),
        "ds_l": c(dsc.reshape(KC, 128).T),
        "wq": c(np.asarray(inputs["L0_attn_wq"], dtype=np.float32)),
        "wk": c(np.asarray(inputs["L0_attn_wk"], dtype=np.float32)),
        "wv": c(np.asarray(inputs["L0_attn_wv"], dtype=np.float32)),
        "wo": c(np.asarray(inputs["L0_attn_wo"], dtype=np.float32)),
        "wmq_t": c(np.asarray(inputs["L0_attn_wmap_q"], dtype=np.float32).T),
        "wmk_t": c(np.asarray(inputs["L0_attn_wmap_k"], dtype=np.float32).T),
        "bq_l": c(np.asarray(inputs["L0_attn_bq"], dtype=np.float32).reshape(H, 128).T),
        "bk_l": c(np.asarray(inputs["L0_attn_bk"], dtype=np.float32).reshape(H, 128).T),
        "bv_r": c(np.asarray(inputs["L0_attn_bv"], dtype=np.float32).reshape(1, D)),
        "bo_r": c(np.asarray(inputs["L0_attn_bo"], dtype=np.float32).reshape(1, D)),
        "ident": np.eye(128, dtype=np.float32),
    }
    in_maps = []
    for core in range(8):
        m = dict(base)
        m["x_in"] = c(xf[core * TOK:(core + 1) * TOK, :])
        in_maps.append(m)
    nc = _get_nc()
    res = run_bass_kernel_spmd(nc, in_maps, core_ids=list(range(8)))
    outs = [np.asarray(res.results[i]["out"]) for i in range(8)]
    try:
        global LAST_X2
        LAST_X2 = np.concatenate([np.asarray(res.results[i]["x2_out"]) for i in range(8)], axis=0)
    except Exception:
        pass
    return np.concatenate(outs, axis=0).reshape(B, S, D).astype(np.float32)
